# version 02
# baseline (speedup 1.0000x reference)
import math

import numpy as np
import ml_dtypes

import concourse.bass as bass
import concourse.mybir as mybir
from concourse.bass_utils import run_bass_kernel_spmd

F32 = mybir.dt.float32
BF16 = mybir.dt.bfloat16
AX = mybir.AxisListType.X
ALU = mybir.AluOpType
AF = mybir.ActivationFunctionType
BF = ml_dtypes.bfloat16

H, HD, D, NTOK = 32, 128, 4096, 2048
NQT = 16
NGRP = 8
NNT = 4
NCH = 32
NE, NR = 8, 16
EPS_R, EPS_L = 1e-6, 1e-5


class Sem:
    __slots__ = ("h", "name", "count")

    def __init__(self, handle, name):
        self.h, self.name, self.count = handle, name, 0


class Plan:

    def __init__(self, nc, stack):
        self.nc, self.stack = nc, stack
        self.ops = {k: [] for k in "tvspy"}
        self._n = 0

    def sem(self, name=None):
        name = name or f"s{self._n}"
        self._n += 1
        return Sem(self.stack.enter_context(self.nc.semaphore(name)), name)

    def op(self, eng, fn, waits=(), inc=None, inc_amt=1, tag=""):
        self.ops[eng].append((fn, list(waits), inc, inc_amt, tag))
        if inc is not None:
            inc.count += inc_amt
            return inc.count
        return None

    def drain(self, eng):
        self.ops[eng].append((lambda e: e.drain(), [], None, 0, "drain"))

    def wait(self, eng, sem, val):
        self.ops[eng].append((None, [(sem, val)], None, 0, "wait"))

    def dma(self, ring, out_ap, in_ap, sem, waits=(), tag=""):
        def fn(eng, o=out_ap, i=in_ap):
            return eng.dma_start(out=o, in_=i)
        return self.op(ring, fn, waits=waits, inc=sem, inc_amt=16, tag=tag)

    def xpose(self, ring, out_ap, in_ap, sem, waits=(), tag=""):
        def fn(eng, o=out_ap, i=in_ap):
            return eng.dma_start(out=o, in_=i, transpose=True)
        return self.op(ring, fn, waits=waits, inc=sem, inc_amt=16, tag=tag)

    def barrier(self, sem, engines="tvspy"):
        target = sem.count + len(engines)
        for k in engines:
            if k in "vsp":
                self.ops[k].append((lambda e: e.drain(), [], None, 0, "bard"))
            self.ops[k].append((lambda e: e.nop(), [], sem, 1, "bar"))
        sem.count = target
        for k in engines:
            self.ops[k].append((None, [(sem, target)], None, 0, "barw"))

    def emit(self):
        with self.nc.Block() as blk:
            def body_for(key):
                oplist = self.ops[key]

                def body(eng):
                    for fn, waits, inc, amt, tag in oplist:
                        for s, v in waits:
                            eng.wait_ge(s.h, v)
                        if fn is None:
                            continue
                        r = fn(eng)
                        if inc is not None:
                            assert r is not None, f"op {tag} returned nothing"
                            r.then_inc(inc.h, amt)
                return body

            blk.tensor(body_for("t"))
            blk.vector(body_for("v"))
            blk.scalar(body_for("s"))
            blk.gpsimd(body_for("p"))
            blk.sync(body_for("y"))


def build_program(n_heads=H, phases="ABCD", taps=(), stack=None):
    import contextlib
    own_stack = stack is None
    stack = stack or contextlib.ExitStack()
    nc = bass.Bass("TRN2", target_bir_lowering=False)
    p = Plan(nc, stack)

    d_xT = nc.dram_tensor("xT", [D, NTOK], F32, kind="ExternalInput")
    d_wq = nc.dram_tensor("wq_t", [H, D, HD], F32, kind="ExternalInput")
    d_wk = nc.dram_tensor("wk_t", [H, D, HD], F32, kind="ExternalInput")
    d_wv = nc.dram_tensor("wv_t", [H // 2, D, 256], F32, kind="ExternalInput")
    d_wo = nc.dram_tensor("wo_t", [NCH, D, 128], F32, kind="ExternalInput")
    d_wb = nc.dram_tensor("wb_t", [NCH, D, 128], F32, kind="ExternalInput")
    d_wg = nc.dram_tensor("wg_t", [128, NCH * NE], F32, kind="ExternalInput")
    d_at = nc.dram_tensor("at_t", [D, 128], F32, kind="ExternalInput")
    d_bm = nc.dram_tensor("bm_t", [128, D], F32, kind="ExternalInput")
    d_bq = nc.dram_tensor("bq_r", [1, D], F32, kind="ExternalInput")
    d_bk = nc.dram_tensor("bk_r", [1, D], F32, kind="ExternalInput")
    d_bv = nc.dram_tensor("bv_r", [1, D], F32, kind="ExternalInput")
    d_bo = nc.dram_tensor("bo_t", [128, NCH], F32, kind="ExternalInput")
    d_bb = nc.dram_tensor("bb_t", [128, NCH], F32, kind="ExternalInput")
    d_gq = nc.dram_tensor("gq_r", [1, HD], F32, kind="ExternalInput")
    d_gk = nc.dram_tensor("gk_r", [1, HD], F32, kind="ExternalInput")
    d_out = nc.dram_tensor("outT", [D, NTOK], F32, kind="ExternalOutput")

    okind = "ExternalOutput" if "o" in taps else "Internal"
    d_oscr = nc.dram_tensor("o_scr", [H, HD, NTOK], BF16, kind=okind)
    z1kind = "ExternalOutput" if "z1" in taps else "Internal"
    d_z1 = nc.dram_tensor("z1_scr", [D, NTOK], F32, kind=z1kind)
    d_y1 = None
    if "y1" in taps:
        d_y1t = nc.dram_tensor("y1_scr", [D, NTOK], BF16,
                               kind="ExternalOutput")
    z2kind = "ExternalOutput" if "z2" in taps else "Internal"
    d_z2 = nc.dram_tensor("z2_scr", [D, NTOK], F32, kind=z2kind)

    ones_row = nc.inline_tensor(np.ones((1, 128), dtype=BF), "c_onesr")
    ones_col = nc.inline_tensor(np.ones((128, 1), dtype=BF), "c_onesc")
    emap_np = np.zeros((NE, 128), dtype=BF)
    for e in range(NE):
        emap_np[e, e * NR:(e + 1) * NR] = 1
    c_emap = nc.inline_tensor(emap_np, "c_emap")
    c_id128 = nc.inline_tensor(np.eye(128, dtype=np.float32), "c_id128")
    c_id8 = nc.inline_tensor(np.eye(8, dtype=np.float32), "c_id8")
    o2a = np.zeros((128, 33), dtype=BF); o2a[:, 0] = 1
    o2b = np.zeros((128, 33), dtype=BF); o2b[:, 32] = 1
    c_ones2a = nc.inline_tensor(o2a, "c_ones2a")
    c_ones2b = nc.inline_tensor(o2b, "c_ones2b")

    dbg = {}

    def tap(name, shape, dtype=F32):
        dbg[name] = nc.dram_tensor("dbg_" + name, shape, dtype,
                                   kind="ExternalOutput")
        return dbg[name]

    A = nc.alloc_sbuf_tensor
    big = A("big", [128, NCH * NTOK], BF16)
    wqb = A("wqb", [128, NCH * HD], BF16)
    wkb = A("wkb", [128, NCH * HD], BF16)
    wvb = A("wvb", [128, NCH * 256], BF16)
    qhb = A("qhb", [128, NTOK], BF16)
    khb = A("khb", [128, NTOK], BF16)
    vblk = A("vblk", [128, 16 * 256], BF16)
    probs0 = A("probs0", [128, NTOK], BF16)
    probs1 = probs0
    probsT = A("probsT", [128, NQT * 256], BF16)
    oevac = A("oevac", [128, 256], BF16)
    q2b = A("q2b", [128, 512], BF16)
    bcsb = A("bcsb", [128, 512], F32)
    nmx = A("nmx", [128, 8], F32)
    lnrow = A("lnrow", [1, 512], F32)
    rstdr = A("rstdr", [1, 512], F32)
    rh_r = A("rh_r", [1, 512], BF16)
    rl_r = A("rl_r", [1, 512], BF16)
    rlrow = A("rlrow", [1, 256], F32)
    rlh_r = A("rlh_r", [1, 256], BF16)
    rll_r = A("rll_r", [1, 256], BF16)
    gq_bf = A("gq_bf", [1, HD], BF16)
    gk_bf = A("gk_bf", [1, HD], BF16)
    biasf = A("biasf", [1, 256], F32)
    bqh = A("bqh", [1, 128], BF16)
    bql = A("bql", [1, 128], BF16)
    bkh = A("bkh", [1, 128], BF16)
    bkl = A("bkl", [1, 128], BF16)
    bvf = A("bvf", [1, 256], F32)
    bvh = A("bvh", [1, 256], BF16)
    bvl = A("bvl", [1, 256], BF16)
    onesr = A("onesr", [1, 128], BF16)
    onesc = A("onesc", [128, 1], BF16)
    ones512 = A("ones512", [1, 512], BF16)
    cst = A("cst", [128, 16], F32)

    ps_qk0 = nc.alloc_psum_tensor("ps_qk0", [128, 512], F32)
    ps_qk1 = nc.alloc_psum_tensor("ps_qk1", [128, 512], F32)
    ps_aux = nc.alloc_psum_tensor("ps_aux", [33, 512], F32)
    ps_bc = nc.alloc_psum_tensor("ps_bc", [128, 512], F32)
    ps_snmA = nc.alloc_psum_tensor("ps_snmA", [128, 1024], F32)
    ps_snmB = nc.alloc_psum_tensor("ps_snmB", [128, 1024], F32)

    s_wd = p.sem("wd")
    s_pe = p.sem("pe")
    s_ac = p.sem("ac")
    s_dv = p.sem("dv")
    s_po = p.sem("po")
    s_yd = p.sem("yd")
    s_sd = p.sem("sd")
    s_bar = p.sem("bar")

    def xchunk(c):
        return big[:, NTOK * c:NTOK * (c + 1)]

    p.dma("y", onesr[:], ones_row[:], s_yd)
    p.dma("y", onesc[:], ones_col[:], s_yd)
    p.dma("y", biasf[:, 0:HD], d_gq[:], s_yd)
    p.dma("y", biasf[:, HD:2 * HD], d_gk[:], s_yd)
    const_ld = s_yd.count
    p.op("v", lambda e: e.memset(ones512[:], 1.0))
    p.drain("v")
    for i, val in enumerate([1.0 / 128, EPS_R, -0.5, -0.5 * math.log(128.0),
                             1.0 / 4096, EPS_L, 0.0, 1.0]):
        p.op("v", lambda e, i=i, val=val: e.memset(cst[:, i:i + 1], val))
    p.drain("v")
    p.op("v", lambda e: e.tensor_copy(out=gq_bf[:], in_=biasf[:, 0:HD]),
         waits=[(s_yd, const_ld)])
    consts_done = p.op(
        "v", lambda e: e.tensor_copy(out=gk_bf[:], in_=biasf[:, HD:2 * HD]),
        inc=s_dv, tag="consts")

    p.wait("t", s_dv, consts_done)
    p.wait("t", s_yd, const_ld)
    p.wait("s", s_dv, consts_done)
    p.wait("p", s_dv, consts_done)

    for i in range(4):
        p.dma("p",
              big[:].rearrange("p (c n) -> p c n", n=NTOK)[:, 8 * i:8 * (i + 1), :],
              d_xT.rearrange("(c p) n -> p c n", p=128)[:, 8 * i:8 * (i + 1), :],
              s_wd, tag=f"xT{i}")
    p.dma("p", nmx[:, 0:4], d_xT[3968:4096, 0:4], s_wd, tag="xTc")
    xT_done = s_wd.count

    st = {
        "wq_rdy": {}, "wk_rdy": {}, "wv_rdy": {}, "bias_rdy": {}, "bv_done": {},
        "pe_q": {}, "pe_k": {}, "pe_v": {}, "bias_used": {},
        "qkfree": {0: None, 1: None},
        "var_done": None,
        "aux_free": None,
        "bc_free": None,
        "bcsb_free": None,
        "rrow_free": None,
        "rlrow_free": None,
        "snm_free": {"A": None, "B": None},
        "probs_free": {0: None, 1: None},
        "nmx_free": {0: None, 1: None},
        "pT_free": None,
        "oev_free": None,
        "qh_done": {}, "vb_done": {},
        "attn_end": {},
    }

    def plan_w_dmas(h):
        w = []
        if h >= 1:
            w.append((s_pe, st["pe_q"][h - 1]))
        p.dma("p", wqb[:].rearrange("p (c f) -> p c f", f=HD),
              d_wq[h].rearrange("(c p) f -> p c f", p=128), s_wd,
              waits=w, tag=f"wq{h}")
        st["wq_rdy"][h] = p.dma(
            "p", wqb[:].rearrange("p (c f) -> p c f", f=HD),
            d_wq[h].rearrange("(c p) f -> p c f", p=128), s_wd, tag=f"wqc{h}")
        w2 = ([(s_dv, st["bias_used"][h - 1])] if h >= 1
              else [(s_dv, consts_done)])
        p.dma("p", biasf[:, 0:128], d_bq[:, HD * h:HD * (h + 1)], s_wd, waits=w2)
        st["bias_rdy"][h] = p.dma(
            "p", biasf[:, 128:256], d_bk[:, HD * h:HD * (h + 1)], s_wd)
        w3 = [(s_pe, st["pe_k"][h - 1])] if h >= 1 else []
        p.dma("p", wkb[:].rearrange("p (c f) -> p c f", f=HD),
              d_wk[h].rearrange("(c p) f -> p c f", p=128), s_wd,
              waits=w3, tag=f"wk{h}")
        st["wk_rdy"][h] = p.dma(
            "p", wkb[:].rearrange("p (c f) -> p c f", f=HD),
            d_wk[h].rearrange("(c p) f -> p c f", p=128), s_wd, tag=f"wkc{h}")
        if h % 2 == 0:
            b = h // 2
            w4 = [(s_pe, st["pe_v"][b - 1])] if b >= 1 else []
            p.dma("p", wvb[:].rearrange("p (c f) -> p c f", f=256),
                  d_wv[b].rearrange("(c p) f -> p c f", p=128), s_wd,
                  waits=w4, tag=f"wv{b}")
            st["wv_rdy"][b] = p.dma(
                "p", wvb[:].rearrange("p (c f) -> p c f", f=256),
                d_wv[b].rearrange("(c p) f -> p c f", p=128), s_wd,
                tag=f"wvc{b}")
            w5 = [(s_pe, st["pe_v"][b - 1])] if b >= 1 else []
            p.dma("p", bvf[:], d_bv[:, 256 * b:256 * (b + 1)], s_wd, waits=w5)
            bvr = s_wd.count
            p.op("p", lambda e: e.tensor_copy(out=bvh[:], in_=bvf[:]),
                 waits=[(s_wd, bvr)])
            p.drain("p")
            st["bv_done"][b] = p.op(
                "p", lambda e: e.tensor_tensor(out=bvl[:], in0=bvf[:],
                                               in1=bvh[:], op=ALU.subtract),
                inc=s_po, tag=f"bv{b}")

    def qkproj(h, which):
        wbuf, wrdy = (wqb, st["wq_rdy"][h]) if which == "q" else \
                     (wkb, st["wk_rdy"][h])
        outb = qhb if which == "q" else khb
        g_row = gq_bf if which == "q" else gk_bf
        bh_r, bl_r = (bqh, bql) if which == "q" else (bkh, bkl)
        off = 0 if which == "q" else 128
        bw = [(s_wd, st["bias_rdy"][h]), (s_dv, consts_done)]
        if h >= 1:
            bw.append((s_pe, st["pe_q" if which == "q" else "pe_k"][h - 1]))
        p.op("v", lambda e: e.tensor_copy(out=bh_r[:], in_=biasf[:, off:off + 128]),
             waits=bw, tag=f"b{which}{h}")
        p.drain("v")
        blo = p.op("v", lambda e: e.tensor_tensor(
            out=bl_r[:], in0=biasf[:, off:off + 128], in1=bh_r[:],
            op=ALU.subtract), inc=s_dv, tag=f"blo{which}{h}")
        if which == "k":
            st["bias_used"][h] = blo

        last_qm = None
        for nt in range(NNT):
            bi = nt % 2
            ps = ps_qk0 if bi == 0 else ps_qk1
            w0 = [(s_wd, wrdy), (s_wd, xT_done), (s_dv, blo)]
            if st["qkfree"][bi] is not None:
                w0.append(st["qkfree"][bi])
            ns = slice(512 * nt, 512 * (nt + 1))
            for c in range(NCH):
                def mm(e, c=c, ps=ps, ns=ns, wbuf=wbuf):
                    return e.matmul(ps[:], wbuf[:, HD * c:HD * (c + 1)],
                                    xchunk(c)[:, ns], start=(c == 0), stop=False)
                p.op("t", mm, waits=w0 if c == 0 else ())
            p.op("t", lambda e, ps=ps: e.matmul(ps[:], bh_r[:], ones512[:],
                                                start=False, stop=False))
            mmd = p.op("t", lambda e, ps=ps: e.matmul(
                ps[:], bl_r[:], ones512[:], start=False, stop=True),
                inc=s_pe, tag=f"{which}p{h}_{nt}")
            sw = [(s_pe, mmd)]
            if st["var_done"] is not None:
                sw.append((s_pe, st["var_done"]))
            sq = p.op("s", lambda e, ps=ps: e.square(out=q2b[:], in_=ps[:]),
                      waits=sw, inc=s_ac, tag=f"sq{which}{h}{nt}")
            aw = [(s_ac, sq)]
            if st["aux_free"] is not None:
                aw.append(st["aux_free"])
            vd = p.op("t", lambda e: e.matmul(ps_aux[0:1, :], onesc[:], q2b[:],
                                              start=True, stop=True),
                      waits=aw, inc=s_pe, tag=f"var{which}{h}{nt}")
            st["var_done"] = vd
            ln = p.op("s", lambda e: e.activation(
                out=lnrow[:], in_=ps_aux[0:1, :], func=AF.Ln, bias=cst[0:1, 1:2],
                scale=cst[0:1, 0:1]), waits=[(s_pe, vd)], inc=s_ac)
            st["aux_free"] = (s_ac, ln)
            p.drain("s")
            fold = cst[0:1, 3:4] if which == "q" else cst[0:1, 6:7]
            ex = p.op("s", lambda e, fold=fold: e.activation(
                out=rstdr[:], in_=lnrow[:], func=AF.Exp, bias=fold,
                scale=cst[0:1, 2:3]), inc=s_ac, tag=f"rstd{which}{h}{nt}")
            rw = [(s_ac, ex)]
            if st["rrow_free"] is not None:
                rw.append(st["rrow_free"])
            p.op("v", lambda e: e.tensor_copy(out=rh_r[:], in_=rstdr[:]),
                 waits=rw)
            p.drain("v")
            rlo = p.op("v", lambda e: e.tensor_tensor(
                out=rl_r[:], in0=rstdr[:], in1=rh_r[:], op=ALU.subtract),
                inc=s_dv)
            bw2 = [(s_dv, rlo)]
            if st["bc_free"] is not None:
                bw2.append(st["bc_free"])
            p.op("t", lambda e, g_row=g_row: e.matmul(
                ps_bc[:], g_row[:], rh_r[:], start=True, stop=False), waits=bw2)
            bcd = p.op("t", lambda e, g_row=g_row: e.matmul(
                ps_bc[:], g_row[:], rl_r[:], start=False, stop=True), inc=s_pe)
            st["rrow_free"] = (s_pe, bcd)
            ew = [(s_pe, bcd)]
            if st["bcsb_free"] is not None:
                ew.append(st["bcsb_free"])
            bce = p.op("s", lambda e: e.copy(out=bcsb[:], in_=ps_bc[:]),
                       waits=ew, inc=s_ac)
            st["bc_free"] = (s_ac, bce)
            qw = [(s_ac, bce)]
            if h >= 1 and nt == 0:
                qw.append((s_pe, st["attn_end"][h - 1]))
            qm = p.op("v", lambda e, ps=ps, outb=outb, ns=ns: e.tensor_tensor(
                out=outb[:, ns], in0=ps[:], in1=bcsb[:], op=ALU.mult),
                waits=qw, inc=s_dv, tag=f"qh{which}{h}{nt}")
            st["qkfree"][bi] = (s_dv, qm)
            st["bcsb_free"] = (s_dv, qm)
            last_qm = qm
        st["qh_done"][(h, which)] = last_qm
        return s_pe.count

    def vproj(b):
        last_ve = None
        for tt in range(16):
            bi = tt % 2
            ps = ps_qk0 if bi == 0 else ps_qk1
            w0 = [(s_wd, st["wv_rdy"][b]), (s_po, st["bv_done"][b]),
                  (s_wd, xT_done)]
            if st["qkfree"][bi] is not None:
                w0.append(st["qkfree"][bi])
            tsl = slice(128 * tt, 128 * (tt + 1))
            for c in range(NCH):
                def mm(e, c=c, ps=ps, tsl=tsl):
                    return e.matmul(ps[:, 0:256], xchunk(c)[:, tsl],
                                    wvb[:, 256 * c:256 * (c + 1)],
                                    start=(c == 0), stop=False)
                p.op("t", mm, waits=w0 if c == 0 else ())
            p.op("t", lambda e, ps=ps: e.matmul(
                ps[:, 0:256], onesr[:], bvh[:], start=False, stop=False))
            vd = p.op("t", lambda e, ps=ps: e.matmul(
                ps[:, 0:256], onesr[:], bvl[:], start=False, stop=True),
                inc=s_pe, tag=f"vp{b}_{tt}")
            vw = [(s_pe, vd)]
            if b >= 1 and tt == 0:
                vw.append((s_pe, st["attn_end"][2 * b - 1]))
            ve = p.op("v", lambda e, ps=ps, tt=tt: e.tensor_copy(
                out=vblk[:, 256 * tt:256 * (tt + 1)], in_=ps[:, 0:256]),
                waits=vw, inc=s_dv, tag=f"ve{b}_{tt}")
            st["qkfree"][bi] = (s_dv, ve)
            last_ve = ve
        st["vb_done"][b] = last_ve
        return s_pe.count

    def attn(h):
        xbar_done = {}
        avp = {}

        def emit_scores(g):
            for qi in range(2):
                qt = 2 * g + qi
                par = qt % 2
                pbuf = probs0 if par == 0 else probs1
                for half, (psn, ps) in enumerate((("A", ps_snmA), ("B", ps_snmB))):
                    w0 = []
                    if st["snm_free"][psn] is not None:
                        w0.append(st["snm_free"][psn])
                    if qt == 0:
                        w0.append((s_dv, st["qh_done"][(h, "q")]))
                        w0.append((s_dv, st["qh_done"][(h, "k")]))
                    for j in range(2):
                        m0 = 1024 * half + 512 * j
                        def mm(e, j=j, m0=m0, qt=qt, ps=ps):
                            return e.matmul(ps[:, 512 * j:512 * (j + 1)],
                                            qhb[:, 128 * qt:128 * (qt + 1)],
                                            khb[:, m0:m0 + 512],
                                            start=True, stop=True)
                        sp = p.op("t", mm, waits=w0 if j == 0 else (),
                                  inc=s_pe if j == 1 else None,
                                  tag=f"snm{h}_{qt}_{half}")
                    nw = [(s_pe, sp)]
                    if st["nmx_free"][par] is not None and half == 0:
                        nw.append(st["nmx_free"][par])
                    p.op("v", lambda e, ps=ps, par=par, half=half: e.reduce_max(
                        out=nmx[:, 2 * par + half:2 * par + half + 1],
                        in_=ps[:], axis=AX, negate=True),
                        waits=nw, tag=f"nm{h}_{qt}_{half}")
                    p.drain("v")
                cmb = p.op("v", lambda e, par=par: e.tensor_tensor(
                    out=nmx[:, 2 * par:2 * par + 1],
                    in0=nmx[:, 2 * par:2 * par + 1],
                    in1=nmx[:, 2 * par + 1:2 * par + 2], op=ALU.min),
                    inc=s_dv, tag=f"nmc{h}_{qt}")
                p.drain("v")
                ew = [(s_dv, cmb)]
                if st["probs_free"][0] is not None:
                    ew.append(st["probs_free"][0])
                for half, ps in ((0, ps_snmA), (1, ps_snmB)):
                    ex = p.op("s", lambda e, ps=ps, half=half, pbuf=pbuf, par=par:
                              e.activation(
                                  out=pbuf[:, 1024 * half:1024 * (half + 1)],
                                  in_=ps[:], func=AF.Exp,
                                  bias=nmx[:, 2 * par:2 * par + 1],
                                  scale=cst[:, 7:8]),
                              waits=ew if half == 0 else (), inc=s_ac,
                              tag=f"exp{h}_{qt}_{half}")
                    st["snm_free"]["A" if half == 0 else "B"] = (s_ac, ex)
                st["nmx_free"][par] = (s_ac, ex)
                ring, rsem = ("y", s_yd) if par == 0 else ("s", s_sd)
                in3 = pbuf[:].rearrange("p (c f) -> p c f", f=128)
                out3 = probsT[:].rearrange("p (c f) -> p c f", f=256)[
                    :, :, 128 * qi:128 * (qi + 1)]
                tw = [(s_ac, ex)]
                if st["pT_free"] is not None and qi == 0:
                    tw.append(st["pT_free"])
                xv = p.xpose(ring, out3, in3, rsem, waits=tw,
                             tag=f"xb{h}_{qt}")
                st["probs_free"][0] = (rsem, xv)
                xbar_done[qt] = (rsem, xv)

        def emit_avsum(g):
            w0 = [xbar_done[2 * g], xbar_done[2 * g + 1]]
            if st["aux_free"] is not None:
                w0 = w0 + [st["aux_free"]]
            for c in range(NQT):
                def sm(e, c=c):
                    return e.matmul(ps_aux[0:1, 0:256], onesc[:],
                                    probsT[:, 256 * c:256 * (c + 1)],
                                    start=(c == 0), stop=(c == NQT - 1))
                smv = p.op("t", sm, waits=w0 if c == 0 else (),
                           inc=s_pe if c == NQT - 1 else None, tag=f"sum{h}_{g}")
            sum_done = smv
            w1 = [(s_dv, st["vb_done"][h // 2])]
            if st["qkfree"][0] is not None:
                w1.append(st["qkfree"][0])
            for c in range(NQT):
                def av(e, c=c):
                    return e.matmul(ps_qk0[:, 0:256],
                                    vblk[:, 256 * c + 128 * (h % 2):
                                         256 * c + 128 * (h % 2) + 128],
                                    probsT[:, 256 * c:256 * (c + 1)],
                                    start=(c == 0), stop=(c == NQT - 1))
                avv = p.op("t", av, waits=w1 if c == 0 else (),
                           inc=s_pe if c == NQT - 1 else None, tag=f"av{h}_{g}")
            st["pT_free"] = (s_pe, avv)
            avp[g] = (avv, sum_done)
            rw = [(s_pe, sum_done)]
            if st["rlrow_free"] is not None:
                rw.append(st["rlrow_free"])
            rc = p.op("v", lambda e: e.reciprocal(out=rlrow[:],
                                                  in_=ps_aux[0:1, 0:256]),
                      waits=rw, inc=s_dv, tag=f"rl{h}_{g}")
            st["aux_free"] = (s_dv, rc)
            p.drain("v")
            p.op("v", lambda e: e.tensor_copy(out=rlh_r[:], in_=rlrow[:]))
            p.drain("v")
            rll = p.op("v", lambda e: e.tensor_tensor(
                out=rll_r[:], in0=rlrow[:], in1=rlh_r[:], op=ALU.subtract),
                inc=s_dv, tag=f"rll{h}_{g}")
            avp[g] = (avv, sum_done, rll)

        def emit_lb_oevac(h, g):
            rll = avp[g][2]
            bw = [(s_dv, rll)]
            if st["bc_free"] is not None:
                bw.append(st["bc_free"])
            p.op("t", lambda e: e.matmul(ps_bc[:, 0:256], onesr[:], rlh_r[:],
                                         start=True, stop=False), waits=bw)
            lbd = p.op("t", lambda e: e.matmul(
                ps_bc[:, 0:256], onesr[:], rll_r[:], start=False, stop=True),
                inc=s_pe, tag=f"lb{h}_{g}")
            st["rlrow_free"] = (s_pe, lbd)
            ew = [(s_pe, lbd)]
            if st["bcsb_free"] is not None:
                ew.append(st["bcsb_free"])
            lbe = p.op("s", lambda e: e.copy(out=bcsb[:, 0:256],
                                             in_=ps_bc[:, 0:256]),
                       waits=ew, inc=s_ac)
            st["bc_free"] = (s_ac, lbe)
            ow = [(s_ac, lbe), (s_pe, avp[g][0])]
            if st["oev_free"] is not None:
                ow.append(st["oev_free"])
            oe = p.op("v", lambda e: e.tensor_tensor(
                out=oevac[:], in0=ps_qk0[:, 0:256], in1=bcsb[:, 0:256],
                op=ALU.mult), waits=ow, inc=s_dv, tag=f"oev{h}_{g}")
            st["qkfree"][0] = (s_dv, oe)
            st["bcsb_free"] = (s_dv, oe)
            od = p.dma("y", d_oscr[h][:, 256 * g:256 * (g + 1)], oevac[:],
                       s_yd, waits=[(s_dv, oe)], tag=f"od{h}_{g}")
            st["oev_free"] = (s_yd, od)
            st.setdefault("odma", {})[h] = od

        for g in range(NGRP):
            if g >= 1:
                emit_avsum(g - 1)
            emit_scores(g)
            if g >= 1:
                emit_lb_oevac(h, g - 1)
        emit_avsum(NGRP - 1)
        emit_lb_oevac(h, NGRP - 1)
        st["attn_end"][h] = s_pe.count

    for h in range(n_heads):
        plan_w_dmas(h)
        st["pe_q"][h] = qkproj(h, "q")
        st["pe_k"][h] = qkproj(h, "k")
        if h % 2 == 0:
            st["pe_v"][h // 2] = vproj(h // 2)
        attn(h)

    if "qkv" in taps:
        t_q = tap("qh", [128, NTOK], BF16); t_k = tap("kh", [128, NTOK], BF16)
        t_v = tap("vb", [128, 16 * 256], BF16)
        w = [(s_pe, s_pe.count), (s_dv, s_dv.count), (s_ac, s_ac.count)]
        p.dma("y", t_q[:], qhb[:], s_yd, waits=w)
        p.dma("y", t_k[:], khb[:], s_yd)
        p.dma("y", t_v[:], vblk[:], s_yd)

    if phases == "AB":
        p.wait("y", s_yd, s_yd.count)
        p.wait("s", s_sd, s_sd.count)
        p.wait("p", s_wd, s_wd.count)
        p.barrier(s_bar)
        p.emit()
        if own_stack:
            stack.close()
        return nc, dbg


    wvb_f = wvb.bitcast(F32)
    wqb_f = wqb.bitcast(F32)
    wkb_f = wkb.bitcast(F32)
    khb_f = khb.bitcast(F32)
    probs0_f = probs0.bitcast(F32)
    probsT_f = probsT.bitcast(F32)
    oevac_f = oevac.bitcast(F32)
    bcsb_bf = bcsb.bitcast(BF16)

    xbuf = [wvb_f[:, 512 * j:512 * (j + 1)] for j in range(3)]
    z1buf = [wvb_f[:, 1536 + 512 * j:2048 + 512 * j] for j in range(2)]
    zbf = [vblk[:, 512 * j:512 * (j + 1)] for j in range(3)]
    zsq = [vblk[:, 1536 + 512 * j:2048 + 512 * j] for j in range(3)]
    z1l = [probs0[:, 512 * j:512 * (j + 1)] for j in range(3)]
    wg_f = bcsb[:, 0:256]
    wg_hi = q2b[:, 0:256]
    wg_lo = q2b[:, 256:512]
    G = probsT_f[0:8, :]
    bocol = oevac_f[:, 0:32]
    bbcol = oevac_f[:, 32:64]
    id8_sb = cst[0:8, 8:16]
    ones2a = probs0[:, 1600:1633]
    ones2b = probs0[:, 1633:1666]
    emap_sb = oevac[0:8, 128:256]
    mb = wvb_f[:, 0:2048]
    rb = wvb_f[:, 2048:4096]
    rowA = probs0[0:1, 0:2048]
    rowB = qhb[0:1, 0:2048]
    rowC = khb[0:1, 0:2048]
    rowD = wkb[0:1, 0:2048]
    logitsT = wkb_f[0:8, 0:2048]

    lgt = bcsb[:, 0:8]
    routeT_hi = probsT[0:8, 0:2048]
    routeT_lo = probsT[0:8, 2048:4096]
    id128_sb = bcsb[:, 128:256]
    khb_f = khb.bitcast(F32)
    zrd_v2 = [wqb_f[:, 0:512], wqb_f[:, 512:1024]]
    zrd_p2 = [wqb_f[:, 1024:1536], wqb_f[:, 1536:2048]]
    tb_v = khb_f[:, 0:512]
    tb_p = khb_f[:, 512:1024]
    atb = vblk
    hwT = qhb
    bmT = probsT
    y1buf = [wvb_f[:, 512 * j:512 * (j + 1)] for j in range(3)]
    z2buf = [khb_f[:, 512 * j:512 * (j + 1)] for j in range(2)]
    zbf2 = [probs0[:, 512 * j:512 * (j + 1)] for j in range(3)]
    zsq2 = [q2b[:, 0:512], bcsb_bf[:, 0:512], bcsb_bf[:, 512:1024]]

    mains = [ps_snmA[:, 0:512], ps_snmA[:, 512:1024], ps_snmB[:, 0:512]]
    gate_ps = ps_snmB[0:8, 512:1024]
    stats_ps = [ps_qk0[0:33, :], ps_qk1[0:33, :], ps_aux[0:33, :],
                ps_bc[0:33, :]]
    mS_ps = ps_snmB[0:8, 0:512]
    S_ps = ps_snmB[0:1, 512:520]
    rtb_ps = ps_snmB[0:8, 640:768]
    h_ps = ps_snmA[:, 0:512]
    rexp_ps = ps_snmA[:, 512:1024]

    hlast = n_heads - 1
    blast = (n_heads - 1) // 2

    for c in range(NCH):
        w = [(s_yd, st["odma"][min(c, hlast)])]
        if c == 0:
            w.append((s_pe, st["pe_k"][hlast]))
        p.dma("p", xchunk(c), d_oscr[min(c, hlast)], s_wd, waits=w,
              tag=f"old{c}")
    p.dma("p", nmx[:, 0:4], d_oscr[hlast][:, 0:4], s_wd, tag="oldc")
    oload = {c: s_wd.count for c in range(NCH)}
    wo_rdy = {}
    wo_rdy[0] = p.dma("p", wqb[:].rearrange("p (c f) -> p c f", f=128),
                      d_wo[0].rearrange("(c p) f -> p c f", p=128), s_wd,
                      waits=[(s_pe, st["pe_q"][hlast])], tag="wo0")
    wo_rdy[1] = p.dma("p", wkb[:].rearrange("p (c f) -> p c f", f=128),
                      d_wo[1].rearrange("(c p) f -> p c f", p=128), s_wd,
                      tag="wo1")
    p.dma("p", wg_f, d_wg[:].rearrange("p (c e) -> p (c e)") if False else
          d_wg[:], s_wd, waits=[st["bcsb_free"]], tag="wg")
    oev_w = [(s_yd, st["odma"][hlast])]
    p.dma("p", bocol, d_bo[:], s_wd, waits=oev_w)
    p.dma("p", bbcol, d_bb[:], s_wd)
    p.dma("p", id8_sb, c_id8[:], s_wd)
    p.dma("p", ones2a, c_ones2a[:], s_wd)
    p.dma("p", ones2b, c_ones2b[:], s_wd)
    p.dma("p", emap_sb, c_emap[:], s_wd)
    p.dma("p", nmx[:, 0:4], d_wg[:, 0:4], s_wd, tag="cldc")
    c_loads = s_wd.count
    qf = [(s_pe, st["attn_end"][hlast])]
    p.op("p", lambda e: e.tensor_copy(out=wg_hi[:], in_=wg_f[:]),
         waits=[(s_wd, c_loads)] + qf)
    p.drain("p")
    wg_cast = p.op("p", lambda e: e.tensor_tensor(
        out=wg_lo[:], in0=wg_f[:], in1=wg_hi[:], op=ALU.subtract),
        inc=s_po, tag="wgcast")

    p.wait("y", s_yd, s_yd.count)
    p.wait("s", s_sd, s_sd.count)
    p.barrier(s_bar)

    cd = {"stt": {}, "zc": {}, "zq": {}, "zd": {}, "xd": {}, "gev": {},
          "mainpe": {}, "statpe": {}, "ocpe": {}}

    def c1_gate_stats(j, gate=True, zbfL=zbf, zsqL=zsq, statL=stats_ps,
                      tagp="c1"):
        ocj, ntj = j // 4, j % 4
        if gate:
            gw = [(s_po, cd["zc"][j]), (s_dv, cd["zlo"][j])]
            if j >= 1:
                gw.append((s_dv, cd["gev"][j - 1]))
            rhs3 = [zbfL[j % 3], z1l[j % 3], zbfL[j % 3]]
            lhs3 = [wg_hi, wg_hi, wg_lo]
            for k in range(3):
                def gm(e, k=k, ocj=ocj, rhs3=rhs3, lhs3=lhs3):
                    return e.matmul(gate_ps,
                                    lhs3[k][:, 8 * ocj:8 * (ocj + 1)],
                                    rhs3[k], start=(k == 0), stop=(k == 2))
                gmd = p.op("t", gm, waits=gw if k == 0 else (),
                           inc=s_pe if k == 2 else None, tag=f"g{tagp}{j}")
            ev_w = [(s_pe, gmd)]
            def gev(e, ocj=ocj, ntj=ntj):
                dst = G[:, 512 * ntj:512 * (ntj + 1)]
                if ocj == 0:
                    return e.tensor_copy(out=dst, in_=gate_ps)
                return e.tensor_tensor(out=dst, in0=dst, in1=gate_ps,
                                       op=ALU.add)
            cd["gev"][j] = p.op("v", gev, waits=ev_w, inc=s_dv,
                                tag=f"gev{tagp}{j}")
        sw = [(s_ac, cd["zq"][j]), (s_po, cd["zc"][j])]
        for k, (l2, r2) in enumerate(((ones2a, zbfL[j % 3]),
                                      (ones2b, zsqL[j % 3]))):
            def sm(e, l2=l2, r2=r2, ocj=ocj, ntj=ntj, k=k, statL=statL):
                return e.matmul(statL[ntj], l2, r2,
                                start=(ocj == 0 and k == 0),
                                stop=(ocj == NCH - 1 and k == 1))
            smd = p.op("t", sm, waits=sw if k == 0 else (),
                       inc=s_pe if k == 1 else None, tag=f"s{tagp}{j}")
        cd["statpe"][j] = smd

    def cd_mainloop(dd, tag, n_oc=NCH):
        for oc in range(n_oc):
            for nt in range(4):
                i = 4 * oc + nt
                if not dd.get("resid_sbuf"):
                    xw = []
                    if i >= 3:
                        xw.append((s_dv, cd["stt"][(tag, i - 3)]))
                    p.dma("s", dd["xbufL"][i % 3],
                          dd["resid"][128 * oc:128 * (oc + 1),
                                      512 * nt:512 * (nt + 1)],
                          s_sd, waits=xw, tag=f"x{tag}{i}")
                    cd["xd"][(tag, i)] = p.dma(
                        "s", dd["xbufL"][i % 3],
                        dd["resid"][128 * oc:128 * (oc + 1),
                                    512 * nt:512 * (nt + 1)],
                        s_sd, tag=f"x{tag}d{i}")
                mw = [(s_wd, dd["w_rdy"][oc]), (s_wd, dd["act_rdy"])]
                if i == 0 and "first_w" in dd:
                    mw = mw + dd["first_w"]
                if i >= 3:
                    mw.append((s_dv, cd["stt"][(tag, i - 3)]))
                wbuf = dd["bufs"][oc % 2]
                for c in range(NCH):
                    def mm(e, c=c, i=i, nt=nt, wbuf=wbuf):
                        return e.matmul(mains[i % 3],
                                        wbuf[:, 128 * c:128 * (c + 1)],
                                        xchunk(c)[:, 512 * nt:512 * (nt + 1)],
                                        start=(c == 0), stop=False)
                    p.op("t", mm, waits=mw if c == 0 else (),
                         tag=f"m{tag}{i}_{c}")
                extra = dd.get("extra_mm")
                if extra is not None:
                    mmd = extra(i)
                else:
                    def fin(e, i=i):
                        return e.matmul(mains[i % 3], onesr[:, 0:1],
                                        ones512[0:1, 0:1], start=False,
                                        stop=True)
                    mmd = None
                if mmd is None:
                    p.ops["t"][-1] = (
                        (lambda e, i=i, wbuf=wbuf: e.matmul(
                            mains[i % 3], wbuf[:, 128 * 31:128 * 32],
                            xchunk(31)[:, 512 * (i % 4):512 * (i % 4 + 1)],
                            start=False, stop=True)),
                        p.ops["t"][-1][1], s_pe, 1, f"m{tag}{i}_31")
                    s_pe.count += 1
                    mmd = s_pe.count
                cd["mainpe"][(tag, i)] = mmd
                if nt == 3:
                    cd["ocpe"][(tag, oc)] = mmd
                if i >= 2:
                    dd["gs"](i - 2)
                zw = [(s_pe, mmd)]
                if not dd.get("resid_sbuf"):
                    zw.append((s_sd, cd["xd"][(tag, i)]))
                if i >= 2:
                    jj = i - 2
                    zw += [(s_yd, cd["zd"][(tag, jj)]),
                           (s_po, cd["zc"][jj if tag == "c" else 1000 + jj]),
                           (s_ac, cd["zq"][jj if tag == "c" else 1000 + jj])]
                if dd.get("resid_sbuf"):
                    resid_ap = xchunk(oc)[:, 512 * nt:512 * (nt + 1)]
                else:
                    resid_ap = dd["xbufL"][i % 3]
                def stt(e, i=i, oc=oc, resid_ap=resid_ap):
                    return e.scalar_tensor_tensor(
                        out=dd["zbufL"][i % 2], in0=mains[i % 3],
                        scalar=dd["bcol"][:, oc:oc + 1],
                        in1=resid_ap, op0=ALU.add, op1=ALU.add)
                p.op("v", stt, waits=zw, tag=f"z{tag}{i}")
                p.drain("v")
                sv = p.op("v", lambda e: e.nop(), inc=s_dv, tag=f"zn{tag}{i}")
                cd["stt"][(tag, i)] = sv
                jkey = i if tag == "c" else 1000 + i
                pw = [(s_dv, sv)]
                if i >= 3:
                    pw.append((s_pe, cd["statpe"][
                        (i - 3) if tag == "c" else 1000 + (i - 3)]))
                p.ops["p"].append((None, [], None, 0, ""))
                p.ops["p"].pop()
                zcp = p.op("p", lambda e, i=i: e.tensor_copy(
                    out=dd["zbfL"][i % 3], in_=dd["zbufL"][i % 2]), waits=pw,
                    inc=s_po, tag=f"zc{tag}{i}")
                zc = zcp
                cd["zc"][jkey] = zc
                if tag == "c":
                    p.drain("v")
                    zlo = p.op("v", lambda e, i=i: e.tensor_tensor(
                        out=z1l[i % 3], in0=dd["zbufL"][i % 2],
                        in1=dd["zbfL"][i % 3], op=ALU.subtract),
                        waits=[(s_po, zcp)], inc=s_dv, tag=f"zlo{i}")
                    cd["zlo"] = cd.get("zlo", {})
                    cd["zlo"][i] = zlo
                aw = [(s_dv, sv)]
                if i >= 3:
                    aw.append((s_pe, cd["statpe"][
                        (i - 3) if tag == "c" else 1000 + (i - 3)]))
                if tag == "d" and i < 3:
                    aw.append(dd["zsq_guard"])
                zq = p.op("s", lambda e, i=i: e.square(
                    out=dd["zsqL"][i % 3], in_=dd["zbufL"][i % 2]),
                    waits=aw, inc=s_ac, tag=f"zq{tag}{i}")
                cd["zq"][jkey] = zq
                p.dma("y", dd["zout"][128 * oc:128 * (oc + 1),
                                      512 * nt:512 * (nt + 1)],
                      dd["zbufL"][i % 2], s_yd, waits=[(s_dv, sv)],
                      tag=f"zd{tag}{i}")
                zd = p.dma("y", nmx[:, 0:4],
                           dd["zout"][128 * oc:128 * (oc + 1),
                                      512 * nt + 508:512 * nt + 512],
                           s_yd, tag=f"zdr{tag}{i}")
                cd["zd"][(tag, i)] = zd
            if oc + 2 < n_oc:
                buf = dd["bufs"][(oc + 2) % 2]
                dw = [(s_pe, cd["ocpe"][(tag, oc)])]
                p.dma("p", buf[:].rearrange("p (c f) -> p c f", f=128),
                      dd["dw"][oc + 2].rearrange("(c p) f -> p c f", p=128),
                      s_wd, waits=dw, tag=f"w{tag}{oc + 2}")
                dd["w_rdy"][oc + 2] = p.dma(
                    "p", buf[:].rearrange("p (c f) -> p c f", f=128),
                    dd["dw"][oc + 2].rearrange("(c p) f -> p c f", p=128),
                    s_wd, tag=f"w{tag}c{oc + 2}")
        dd["gs"](4 * n_oc - 2)
        dd["gs"](4 * n_oc - 1)

    ddc = {
        "w_rdy": wo_rdy, "dw": d_wo, "act_rdy": xT_done, "resid": d_xT,
        "xbufL": xbuf, "zbufL": z1buf, "zbfL": zbf, "zsqL": zsq,
        "bcol": bocol, "zout": d_z1,
        "gs": lambda j: c1_gate_stats(j, gate=True),
        "extra_mm": None,
    }
    ddc["act_rdy"] = c_loads
    ddc["bufs"] = (wqb, wkb)
    cd_mainloop(ddc, "c")
    c1_end_pe = s_pe.count
    c1_gate_last = cd["gev"][127]
    c1_stats_last = cd["statpe"][127]

    p.wait("y", s_yd, s_yd.count)
    p.wait("s", s_sd, s_sd.count)
    p.wait("p", s_wd, s_wd.count)
    p.barrier(s_bar)

    if phases == "ABC1":
        p.wait("s", s_sd, s_sd.count)
        p.wait("p", s_wd, s_wd.count)
        p.emit()
        if own_stack:
            stack.close()
        return nc, dbg

    def stats_finalize(statL, rA, rB, rC, rD, mbT, rbT, tagp):
        bc_rot = None
        for nt in range(4):
            sl = slice(512 * nt, 512 * (nt + 1))
            p.op("v", lambda e, nt=nt: e.tensor_scalar(
                out=lnrow[:], in0=statL[nt][0:1, :], scalar1=1.0 / 4096,
                scalar2=None, op0=ALU.mult), tag=f"mf{tagp}{nt}")
            p.drain("v")
            p.op("v", lambda e, sl=sl: e.tensor_copy(out=rA[:, sl],
                                                     in_=lnrow[:]))
            p.drain("v")
            p.op("v", lambda e, sl=sl: e.tensor_tensor(
                out=rB[:, sl], in0=lnrow[:], in1=rA[:, sl], op=ALU.subtract))
            p.drain("v")
            p.op("v", lambda e: e.tensor_tensor(
                out=rstdr[:], in0=lnrow[:], in1=lnrow[:], op=ALU.mult))
            p.drain("v")
            vv = p.op("v", lambda e, nt=nt: e.scalar_tensor_tensor(
                out=lnrow[:], in0=statL[nt][32:33, :], scalar=1.0 / 4096,
                in1=rstdr[:], op0=ALU.mult, op1=ALU.subtract),
                inc=s_dv, tag=f"var{tagp}{nt}")
            ln2 = p.op("s", lambda e: e.activation(
                out=rstdr[:], in_=lnrow[:], func=AF.Ln, bias=cst[0:1, 5:6],
                scale=cst[0:1, 7:8]), waits=[(s_dv, vv)], inc=s_ac)
            p.drain("s")
            r2 = p.op("s", lambda e: e.activation(
                out=lnrow[:], in_=rstdr[:], func=AF.Exp, bias=cst[0:1, 6:7],
                scale=cst[0:1, 2:3]), inc=s_ac, tag=f"rstd2{tagp}{nt}")
            p.op("v", lambda e, sl=sl: e.tensor_copy(out=rC[:, sl],
                                                     in_=lnrow[:]),
                 waits=[(s_ac, r2)])
            p.drain("v")
            rd = p.op("v", lambda e, sl=sl: e.tensor_tensor(
                out=rD[:, sl], in0=lnrow[:], in1=rC[:, sl], op=ALU.subtract),
                inc=s_dv, tag=f"rlo2{tagp}{nt}")
            bc2 = ps_snmB[:, 0:512]
            for dst, r1, r2_ in ((mbT, rA, rB), (rbT, rC, rD)):
                bw = [(s_dv, rd)]
                if bc_rot is not None:
                    bw.append((s_ac, bc_rot))
                p.op("t", lambda e, r1=r1, sl=sl, bc2=bc2: e.matmul(
                    bc2, onesr[:], r1[:, sl], start=True, stop=False),
                    waits=bw)
                bd = p.op("t", lambda e, r2_=r2_, sl=sl, bc2=bc2: e.matmul(
                    bc2, onesr[:], r2_[:, sl], start=False, stop=True),
                    inc=s_pe)
                bc_rot = p.op("s", lambda e, dst=dst, sl=sl, bc2=bc2: e.copy(
                    out=dst[:, sl], in_=bc2), waits=[(s_pe, bd)],
                    inc=s_ac, tag=f"bc{tagp}{nt}")
        return bc_rot

    p.dma("p", id128_sb, c_id128[:], s_wd, waits=[(s_po, wg_cast)],
          tag="id128")
    id128_rdy = p.dma("p", id128_sb, c_id128[:], s_wd, tag="id128c")
    if "gate" in taps:
        t_g0 = tap("g0", [8, NTOK])
        p.dma("y", t_g0[:], G[:, :], s_yd, waits=[(s_dv, c1_gate_last)],
              tag="tap_g0")

    mbrb_done = stats_finalize(stats_ps, rowA, rowB, rowC, rowD, mb, rb, "c")

    if phases == "ABC2s":
        p.wait("y", s_yd, s_yd.count)
        p.wait("s", s_sd, s_sd.count)
        p.wait("p", s_wd, s_wd.count)
        p.barrier(s_bar)
        p.emit()
        if own_stack:
            stack.close()
        return nc, dbg

    for oc in range(NCH):
        for k, wgx in enumerate((wg_hi, wg_lo)):
            def sm2(e, oc=oc, wgx=wgx):
                return e.matmul(S_ps, onesc[:], wgx[:, 8 * oc:8 * (oc + 1)],
                                start=(oc == 0 and wgx is wg_hi),
                                stop=(oc == NCH - 1 and wgx is wg_lo))
            sS = p.op("t", sm2, inc=s_pe if (oc == NCH - 1 and k == 1) else
                      None, tag=f"S{oc}")
    if phases == "ABC2g1":
        p.wait("y", s_yd, s_yd.count)
        p.wait("s", s_sd, s_sd.count)
        p.wait("p", s_wd, s_wd.count)
        p.wait("t", s_pe, s_pe.count)
        p.barrier(s_bar)
        p.emit()
        if own_stack:
            stack.close()
        return nc, dbg
    S_hi = rlh_r[0:1, 0:8]
    S_lo = rll_r[0:1, 0:8]
    Srow = rstdr[0:1, 0:8]
    p.op("v", lambda e: e.tensor_copy(out=Srow, in_=S_ps), waits=[(s_pe, sS)])
    p.drain("v")
    p.op("v", lambda e: e.tensor_copy(out=S_hi, in_=Srow))
    p.drain("v")
    shl = p.op("v", lambda e: e.tensor_tensor(out=S_lo, in0=Srow, in1=S_hi,
                                              op=ALU.subtract), inc=s_dv,
               tag="Slo")
    if phases == "ABC2g2":
        p.wait("y", s_yd, s_yd.count)
        p.wait("v", s_dv, s_dv.count)
        p.barrier(s_bar)
        p.emit()
        if own_stack:
            stack.close()
        return nc, dbg
    lg_tp_last = None
    msub_prev = None
    for nt in range(4):
        sl = slice(512 * nt, 512 * (nt + 1))
        mw = [(s_dv, shl), (s_ac, mbrb_done)]
        if msub_prev is not None:
            mw.append((s_dv, msub_prev))
        trips = ((S_hi, rowA), (S_hi, rowB), (S_lo, rowA))
        for k, (sr, mr) in enumerate(trips):
            def msm(e, sr=sr, mr=mr, sl=sl, k=k):
                return e.matmul(mS_ps, sr, mr[:, sl], start=(k == 0),
                                stop=(k == 2))
            msd = p.op("t", msm, waits=mw if k == 0 else (),
                       inc=s_pe if k == 2 else None, tag=f"mS{nt}")
        lw = [(s_pe, msd), (s_dv, c1_gate_last)]
        msub_prev = p.op("v", lambda e, sl=sl: e.tensor_tensor(
            out=G[:, sl], in0=G[:, sl], in1=mS_ps, op=ALU.subtract),
            waits=lw, inc=s_dv, tag=f"lsub{nt}")
        p.drain("v")
        lgd = p.op("v", lambda e, sl=sl: e.tensor_tensor(
            out=logitsT[:, sl], in0=G[:, sl], in1=rb[0:8, sl],
            op=ALU.mult), waits=[(s_ac, mbrb_done)], inc=s_dv,
            tag=f"lg{nt}")

    if phases == "ABC2g":
        p.wait("y", s_yd, s_yd.count)
        p.wait("s", s_sd, s_sd.count)
        p.wait("p", s_wd, s_wd.count)
        p.barrier(s_bar)
        p.emit()
        if own_stack:
            stack.close()
        return nc, dbg

    if "gate" in taps:
        t_lg0 = tap("lg0", [8, NTOK])
        p.dma("y", t_lg0[:], logitsT[:, :], s_yd, waits=[(s_dv, lgd)],
              tag="tap_lg0")
    rtd = {}
    for q in range(NQT):
        tw = [(s_dv, lgd)]
        tp1 = p.op("t", lambda e, q=q: e.transpose(
            ps_bc[:, 0:8], logitsT[:, 128 * q:128 * (q + 1)], id8_sb),
            waits=tw, inc=s_pe, tag=f"rt1{q}")
        lg_tp_last = tp1
        p.drain("s")
        ev = p.op("s", lambda e: e.copy(out=lgt, in_=ps_bc[:, 0:8]),
                  waits=[(s_pe, tp1)], inc=s_ac)
        m1c = bcsb[:, 8:9]
        nm1c = bcsb[:, 9:10]
        m2c = bcsb[:, 10:11]
        e2c = bcsb[:, 11:12]
        denc = bcsb[:, 12:13]
        eq1 = bcsb[:, 16:24]
        lg2 = bcsb[:, 24:32]
        msk = bcsb[:, 32:40]
        ex8 = bcsb[:, 40:48]
        rte = bcsb[:, 56:64]
        p.op("v", lambda e: e.reduce_max(out=m1c, in_=lgt, axis=AX),
             waits=[(s_ac, ev)])
        p.drain("v")
        p.op("v", lambda e: e.reduce_max(out=nm1c, in_=lgt, axis=AX,
                                         negate=True))
        p.drain("v")
        p.op("v", lambda e: e.tensor_scalar(
            out=eq1, in0=lgt, scalar1=m1c, scalar2=None, op0=ALU.is_equal))
        p.drain("v")
        p.op("v", lambda e: e.scalar_tensor_tensor(
            out=lg2, in0=eq1, scalar=-1e30, in1=lgt, op0=ALU.mult,
            op1=ALU.add))
        p.drain("v")
        p.op("v", lambda e: e.reduce_max(out=m2c, in_=lg2, axis=AX))
        p.drain("v")
        p.op("v", lambda e: e.tensor_scalar(
            out=msk, in0=lg2, scalar1=m2c, scalar2=None, op0=ALU.is_equal))
        p.drain("v")
        mkd = p.op("v", lambda e: e.tensor_tensor(
            out=msk, in0=msk, in1=eq1, op=ALU.add), inc=s_dv)
        exd = p.op("s", lambda e: e.activation(
            out=ex8, in_=lgt, func=AF.Exp, bias=nm1c, scale=cst[:, 7:8]),
            waits=[(s_dv, mkd)], inc=s_ac)
        e2d = p.op("s", lambda e: e.activation(
            out=e2c, in_=m2c, func=AF.Exp, bias=nm1c, scale=cst[:, 7:8]),
            inc=s_ac)
        p.op("v", lambda e: e.tensor_scalar(
            out=denc, in0=e2c, scalar1=1.0, scalar2=None, op0=ALU.add),
            waits=[(s_ac, e2d)])
        p.drain("v")
        p.op("v", lambda e: e.reciprocal(out=m1c, in_=denc))
        p.drain("v")
        p.op("v", lambda e: e.tensor_tensor(out=ex8, in0=ex8, in1=msk,
                                            op=ALU.mult))
        p.drain("v")
        rdd = p.op("v", lambda e: e.tensor_scalar(
            out=rte, in0=ex8, scalar1=m1c, scalar2=None, op0=ALU.mult),
            inc=s_dv, tag=f"rte{q}")
        tp2 = p.op("t", lambda e: e.transpose(rtb_ps, rte, id128_sb),
                   waits=[(s_dv, rdd), (s_wd, id128_rdy)], inc=s_pe,
                   tag=f"rt2{q}")
        p.op("v", lambda e, q=q: e.tensor_copy(
            out=routeT_hi[:, 128 * q:128 * (q + 1)], in_=rtb_ps),
            waits=[(s_pe, tp2)])
        p.drain("v")
        rtf = p.op("v", lambda e, q=q: e.scalar_tensor_tensor(
            out=routeT_lo[:, 128 * q:128 * (q + 1)], in0=rtb_ps, scalar=1.0,
            in1=routeT_hi[:, 128 * q:128 * (q + 1)], op0=ALU.mult,
            op1=ALU.subtract), inc=s_dv, tag=f"rtf{q}")
        rtd[q] = rtf
    route_done = rtd[NQT - 1]
    lg_tp_last2 = s_pe.count
    if "gate" in taps:
        t_wg = tap("wg", [128, 512], BF16)
        t_zl = tap("zl", [128, 2048], BF16)
        t_zb = tap("zb", [128, 1536], BF16)
        p.dma("y", t_wg[:], q2b[:], s_yd, waits=[(s_dv, route_done)],
              tag="tap_wg")
        p.dma("y", t_zl[:], probs0[:], s_yd, tag="tap_zl")
        p.dma("y", t_zb[:], vblk[:, 0:1536], s_yd, tag="tap_zb")
        t_lg = tap("lg", [8, NTOK])
        t_rt = tap("rt", [8, NTOK], BF16)
        t_g = tap("g", [8, NTOK])
        p.dma("y", t_lg[:], logitsT[:, :], s_yd,
              waits=[(s_dv, route_done)], tag="tap_lg")
        p.dma("y", t_rt[:], routeT_hi[:, :], s_yd, tag="tap_rt")
        p.dma("y", t_g[:], G[:, :], s_yd, tag="tap_g")


    if phases == "ABC2r":
        p.wait("y", s_yd, s_yd.count)
        p.wait("s", s_sd, s_sd.count)
        p.wait("p", s_wd, s_wd.count)
        p.barrier(s_bar)
        p.emit()
        if own_stack:
            stack.close()
        return nc, dbg

    atb_w = [(s_pe, c1_stats_last)]
    p.dma("p", atb[:].rearrange("p (c f) -> p c f", f=128),
          d_at.rearrange("(c p) f -> p c f", p=128), s_wd,
          waits=atb_w, tag="atb")
    atb_rdy = p.dma("p", nmx[:, 0:4], d_at[3968:4096, 0:4], s_wd, tag="atbc")
    wb_rdy = {}

    c2n = {"en": {}, "zd": {}}
    for oc in range(NCH):
        eng = "v" if oc % 2 == 0 else "p"
        es = s_dv if eng == "v" else s_po
        k = oc // 2
        zr = wqb_f[:, 0:2048] if eng == "v" else wkb_f[:, 0:2048]
        dw = [(s_pe, c1_end_pe)]
        if eng == "p":
            dw.append((s_pe, lg_tp_last2))
        if k >= 1:
            dw.append((es, c2n["en"][(eng, k - 1)]))
            dw.append((s_sd, c2n["zd"][(eng, k - 1)]))
        p.dma("s", zr, d_z1[128 * oc:128 * (oc + 1), :], s_sd, waits=dw,
              tag=f"c2z{oc}")
        zd = p.dma("s", zr, d_z1[128 * oc:128 * (oc + 1), :], s_sd,
                   tag=f"c2zd{oc}")
        c2n["zd"][(eng, k)] = zd
        p.op(eng, lambda e, zr=zr: e.tensor_tensor(
            out=zr, in0=zr, in1=mb[:], op=ALU.subtract),
            waits=[(s_sd, zd), (s_ac, mbrb_done)], tag=f"c2s{oc}")
        p.drain(eng)
        ev = p.op(eng, lambda e, zr=zr, oc=oc: e.tensor_tensor(
            out=xchunk(oc), in0=zr, in1=rb[:], op=ALU.mult), inc=es,
            tag=f"c2n{oc}")
        c2n["en"][(eng, k)] = ev
    c2_end = {"v": s_dv.count, "p": s_po.count, "y": s_yd.count,
              "s": s_sd.count}
    p.dma("p", wkb[:].rearrange("p (c f) -> p c f", f=128),
          d_wb[0].rearrange("(c p) f -> p c f", p=128), s_wd,
          waits=[(s_pe, lg_tp_last), (s_dv, c2_end["v"]),
                 (s_po, c2_end["p"]), (s_sd, c2_end["s"])], tag="wb0")
    wb_rdy[0] = p.dma("p", wkb[:].rearrange("p (c f) -> p c f", f=128),
                      d_wb[0].rearrange("(c p) f -> p c f", p=128), s_wd,
                      tag="wb0c")
    if "y1" in taps:
        for c in range(NCH):
            p.dma("y", d_y1t[128 * c:128 * (c + 1), :], xchunk(c), s_yd,
                  waits=[(s_dv, c2_end["v"]), (s_po, c2_end["p"])],
                  tag=f"y1tap{c}")
    p.dma("p", wqb[:].rearrange("p (c f) -> p c f", f=128),
          d_wb[1].rearrange("(c p) f -> p c f", p=128), s_wd,
          waits=[(s_dv, c2_end["v"]), (s_po, c2_end["p"]),
                 (s_yd, c2_end["y"]), (s_sd, c2_end["s"])], tag="wb1")
    wb_rdy[1] = p.dma("p", nmx[:, 0:4], d_wb[1][3968:4096, 0:4], s_wd,
                      tag="wb1c")

    p.wait("y", s_yd, s_yd.count)
    p.wait("s", s_sd, s_sd.count)
    p.wait("p", s_wd, s_wd.count)
    p.barrier(s_bar)

    if phases == "ABC2":
        p.wait("s", s_sd, s_sd.count)
        p.wait("p", s_wd, s_wd.count)
        p.emit()
        if own_stack:
            stack.close()
        return nc, dbg

    bm_rdy = None
    hw_done = {}
    for nt in range(4):
        sl = slice(512 * nt, 512 * (nt + 1))
        hw0 = [(s_wd, atb_rdy)]
        if nt >= 1:
            hw0.append((s_dv, hw_done[nt - 1]))
        for c in range(NCH):
            def hm(e, c=c, sl=sl):
                return e.matmul(h_ps, atb[:, 128 * c:128 * (c + 1)],
                                xchunk(c)[:, sl], start=(c == 0),
                                stop=(c == NCH - 1))
            hmd = p.op("t", hm, waits=hw0 if c == 0 else (),
                       inc=s_pe if c == NCH - 1 else None, tag=f"hm{nt}")
        rw = [(s_dv, route_done)]
        if nt >= 1:
            rw.append((s_ac, re_prev))
        for k, rt in enumerate((routeT_hi, routeT_lo)):
            def rm(e, rt=rt, sl=sl, k=k):
                return e.matmul(rexp_ps, emap_sb, rt[:, sl], start=(k == 0),
                                stop=(k == 1))
            rmd = p.op("t", rm, waits=rw if k == 0 else (),
                       inc=s_pe if k == 1 else None, tag=f"rm{nt}")
        rew = [(s_pe, rmd)]
        if nt >= 1:
            rew.append((s_dv, hw_done[nt - 1]))
        re = p.op("s", lambda e: e.copy(out=bcsb[:, 0:512], in_=rexp_ps),
                  waits=rew, inc=s_ac)
        re_prev = re
        hw = p.op("v", lambda e, sl=sl: e.tensor_tensor(
            out=hwT[:, sl], in0=h_ps, in1=bcsb[:, 0:512], op=ALU.mult),
            waits=[(s_ac, re), (s_pe, hmd)], inc=s_dv, tag=f"hw{nt}")
        hw_done[nt] = hw
    if phases == "ABDpre":
        p.wait("y", s_yd, s_yd.count)
        p.wait("s", s_sd, s_sd.count)
        p.wait("p", s_wd, s_wd.count)
        p.wait("v", s_dv, s_dv.count)
        p.barrier(s_bar)
        p.emit()
        if own_stack:
            stack.close()
        return nc, dbg
    re_prev_final = re_prev
    p.dma("p", bmT[:], d_bm[:], s_wd, waits=[(s_pe, rmd)], tag="bmT")
    bm_rdy = p.dma("p", nmx[:, 0:4], d_bm[:, 0:4], s_wd, tag="bmTc")

    def d1_gate_stats(j):
        c1_gate_stats(j, gate=False, zbfL=zbf2, zsqL=zsq2, statL=stats_ps,
                      tagp="d")

    def d1_extra(i):
        def dm(e, i=i):
            return e.matmul(mains[i % 3],
                            bmT[:, 128 * (i // 4):128 * (i // 4 + 1)],
                            hwT[:, 512 * (i % 4):512 * (i % 4 + 1)],
                            start=False, stop=True)
        w = [(s_wd, bm_rdy), (s_dv, hw_done[3])]
        return p.op("t", dm, waits=w, inc=s_pe, tag=f"dm{i}")

    ddd = {
        "w_rdy": wb_rdy, "dw": d_wb, "act_rdy": atb_rdy, "resid": d_y1,
        "xbufL": y1buf, "zbufL": z2buf, "zbfL": zbf2, "zsqL": zsq2,
        "bcol": bbcol, "zout": d_z2, "resid_sbuf": True,
        "gs": d1_gate_stats,
        "extra_mm": d1_extra,
        "bufs": (wkb, wqb),
        "first_w": [(s_dv, hw_done[3]), (s_ac, re_prev_final)],
        "zsq_guard": (s_dv, hw_done[3]),
    }
    p.dma("p", ones2a, c_ones2a[:], s_wd,
          waits=[(s_dv, c2_end["v"]), (s_po, c2_end["p"]),
                 (s_yd, c2_end["y"]), (s_sd, s_sd.count)])
    p.dma("p", ones2b, c_ones2b[:], s_wd)
    o2rdy = p.dma("p", nmx[:, 0:4], d_wg[:, 0:4], s_wd, tag="o2c")
    ddd["first_w"].append((s_wd, o2rdy))

    def d1_gs_fix(j, n_oc=NCH):
        ocj, ntj = j // 4, j % 4
        sw = [(s_ac, cd["zq"][1000 + j]), (s_po, cd["zc"][1000 + j])]
        for k, (l2, r2) in enumerate(((ones2a, zbf2[j % 3]),
                                      (ones2b, zsq2[j % 3]))):
            def sm(e, l2=l2, r2=r2, ocj=ocj, ntj=ntj, k=k, n_oc=n_oc):
                return e.matmul(stats_ps[ntj], l2, r2,
                                start=(ocj == 0 and k == 0),
                                stop=(ocj == n_oc - 1 and k == 1))
            smd = p.op("t", sm, waits=sw if k == 0 else (),
                       inc=s_pe if k == 1 else None, tag=f"sd{j}")
        cd["statpe"][1000 + j] = smd
    n_oc_d = 2 if phases == "ABD1a" else NCH
    ddd["gs"] = lambda j: d1_gs_fix(j, n_oc=n_oc_d)
    cd_mainloop(ddd, "d", n_oc=n_oc_d)
    d1_stats_last = cd["statpe"][1000 + 4 * n_oc_d - 1]

    p.wait("y", s_yd, s_yd.count)
    p.wait("s", s_sd, s_sd.count)
    p.barrier(s_bar)

    if phases in ("ABD1", "ABD1a"):
        p.wait("s", s_sd, s_sd.count)
        p.wait("p", s_wd, s_wd.count)
        p.emit()
        if own_stack:
            stack.close()
        return nc, dbg

    rowA2 = probs0[0:1, 0:2048]
    rowB2 = qhb[0:1, 0:2048]
    rowC2 = khb[0:1, 0:2048]
    rowD2 = vblk[0:1, 0:2048]
    mbrb2_done = stats_finalize(stats_ps, rowA2, rowB2, rowC2, rowD2, mb, rb,
                                "d")
    d2n = {"en": {}, "zd": {}, "yd": {}}
    d1_pe_end = s_pe.count
    for oc in range(NCH):
        eng = "v" if oc % 2 == 0 else "p"
        es = s_dv if eng == "v" else s_po
        k = oc // 2
        zr = wqb_f[:, 0:2048] if eng == "v" else wkb_f[:, 0:2048]
        dw = [(s_pe, d1_pe_end)]
        if k >= 1:
            dw.append((es, d2n["en"][(eng, k - 1)]))
            dw.append((s_sd, d2n["zd"][(eng, k - 1)]))
            dw.append((s_yd, d2n["yd"][(eng, k - 1)]))
        p.dma("s", zr, d_z2[128 * oc:128 * (oc + 1), :], s_sd, waits=dw,
              tag=f"d2z{oc}")
        zd = p.dma("s", zr, d_z2[128 * oc:128 * (oc + 1), :], s_sd,
                   tag=f"d2zd{oc}")
        d2n["zd"][(eng, k)] = zd
        p.op(eng, lambda e, zr=zr: e.tensor_tensor(
            out=zr, in0=zr, in1=mb[:], op=ALU.subtract),
            waits=[(s_sd, zd), (s_ac, mbrb2_done)], tag=f"d2s{oc}")
        p.drain(eng)
        p.op(eng, lambda e, zr=zr: e.tensor_tensor(
            out=zr, in0=zr, in1=rb[:], op=ALU.mult), tag=f"d2m{oc}")
        p.drain(eng)
        ev = p.op(eng, lambda e: e.nop(), inc=es, tag=f"d2nn{oc}")
        d2n["en"][(eng, k)] = ev
        p.dma("y", d_out[128 * oc:128 * (oc + 1), :], zr, s_yd,
              waits=[(es, ev)], tag=f"out{oc}")
        d2n["yd"][(eng, k)] = p.dma(
            "y", nmx[:, 0:4], d_out[128 * oc:128 * (oc + 1), 2044:2048],
            s_yd, tag=f"outr{oc}")
    p.wait("y", s_yd, s_yd.count)
    p.wait("s", s_sd, s_sd.count)
    p.wait("p", s_wd, s_wd.count)
    p.barrier(s_bar)

    p.emit()
    if own_stack:
        stack.close()
    return nc, dbg



def _layouts(inputs):
    x = inputs["L0_attn_hidden_states"]
    Wq, Wk, Wv = (inputs["L0_attn_Wq"], inputs["L0_attn_Wk"],
                  inputs["L0_attn_Wv"])
    Wo = inputs["L0_attn_Wo"]
    Wb, Wg = inputs["L0_ffn_W_base"], inputs["L0_ffn_W_gate"]
    Amat, Bmat = inputs["L0_ffn_A"], inputs["L0_ffn_B"]
    shared = {
        "wq_t": np.ascontiguousarray(Wq.reshape(D, H, HD).transpose(1, 0, 2)),
        "wk_t": np.ascontiguousarray(Wk.reshape(D, H, HD).transpose(1, 0, 2)),
        "wv_t": np.ascontiguousarray(
            Wv.reshape(D, H // 2, 256).transpose(1, 0, 2)),
        "wo_t": np.ascontiguousarray(Wo.reshape(D, NCH, 128).transpose(1, 0, 2)),
        "wb_t": np.ascontiguousarray(
            Wb.T.reshape(D, NCH, 128).transpose(1, 0, 2)),
        "wg_t": np.ascontiguousarray(
            Wg.T.reshape(NCH, 128, NE).transpose(1, 0, 2).reshape(128, NCH * NE)),
        "at_t": np.ascontiguousarray(Amat.reshape(NE * NR, D).T),
        "bm_t": np.ascontiguousarray(Bmat.transpose(0, 2, 1).reshape(NE * NR, D)),
        "bq_r": np.ascontiguousarray(inputs["L0_attn_bq"].reshape(1, D)),
        "bk_r": np.ascontiguousarray(inputs["L0_attn_bk"].reshape(1, D)),
        "bv_r": np.ascontiguousarray(inputs["L0_attn_bv"].reshape(1, D)),
        "bo_t": np.ascontiguousarray(inputs["L0_attn_bo"].reshape(NCH, 128).T),
        "bb_t": np.ascontiguousarray(inputs["L0_ffn_b_base"].reshape(NCH, 128).T),
        "gq_r": np.ascontiguousarray(inputs["L0_attn_gq"].reshape(1, HD)),
        "gk_r": np.ascontiguousarray(inputs["L0_attn_gk"].reshape(1, HD)),
    }
    per_core = []
    for b in range(x.shape[0]):
        m = dict(shared)
        m["xT"] = np.ascontiguousarray(x[b].T)
        per_core.append(m)
    return per_core


def kernel(**inputs):
    nc, _ = build_program(n_heads=H, phases="ABCD")
    in_maps = _layouts(inputs)
    res = run_bass_kernel_spmd(nc, in_maps, core_ids=list(range(8)))
    return np.stack([np.ascontiguousarray(r["outT"].T) for r in res.results])
